# version 17
# baseline (speedup 1.0000x reference)
import contextlib
import numpy as np
import ml_dtypes

import concourse.bass as bass
import concourse.mybir as mybir
from concourse.bass_utils import run_bass_kernel_spmd

F32 = mybir.dt.float32
BF16 = mybir.dt.bfloat16
FP16 = mybir.dt.float16
FP8 = mybir.dt.float8e4
AF = mybir.ActivationFunctionType
ALU = mybir.AluOpType

S, D, HD, H = 16384, 1024, 64, 4096
NCORES = 8
TPC = 16
ROWS = TPC * 128
NCH = S // 128
DB = D // 128
HB = H // 128


def _pieces(lo, hi):
    out = []
    a = lo
    while a < hi:
        b = min((a // 512 + 1) * 512, hi)
        out.append((a, b))
        a = b
    return out


def _halves(kp):
    lo = 128 * (kp >> 3)
    if lo < 1024:
        return [(lo, 1024, 'A', 0), (1024, 2048, 'B', 1024)]
    return [(lo, 2048, 'A' if (kp & 1) else 'B', 1024)]


class Sem:
    def __init__(self, stack, nc, name):
        self.h = stack.enter_context(nc.semaphore(name))
        self.n = 0

    def inc(self, instr, k=1):
        instr.then_inc(self.h, k)
        self.n += k
        return instr


def build_nc():
    nc = bass.Bass("TRN2", target_bir_lowering=False)

    d_xprod = nc.dram_tensor("x_prod", [ROWS, D], F32, kind="ExternalInput").ap()
    d_xown = nc.dram_tensor("x_own", [ROWS, D], F32, kind="ExternalInput").ap()
    d_wq = nc.dram_tensor("wq", [D, HD], F32, kind="ExternalInput").ap()
    d_wk = nc.dram_tensor("wk", [D, HD], F32, kind="ExternalInput").ap()
    d_wv1 = nc.dram_tensor("wv1", [D, HD], F32, kind="ExternalInput").ap()
    d_wv2 = nc.dram_tensor("wv2", [HD, D], F32, kind="ExternalInput").ap()
    d_w1t = nc.dram_tensor("w1t", [128, D * H // 128], F32, kind="ExternalInput").ap()
    d_b1 = nc.dram_tensor("b1r", [128, HB], F32, kind="ExternalInput").ap()
    d_w2t = nc.dram_tensor("w2t", [128, D * H // 128], F32, kind="ExternalInput").ap()
    d_b2 = nc.dram_tensor("b2r", [1, D], F32, kind="ExternalInput").ap()
    d_mask = nc.dram_tensor("maskc", [128, 8, 128], BF16, kind="ExternalInput").ap()
    d_dm = nc.dram_tensor("dm", [ROWS, D], BF16, kind="ExternalInput").ap()
    d_id = nc.dram_tensor("ident", [128, 128], F32, kind="ExternalInput").ap()
    d_out = nc.dram_tensor("out", [ROWS, D], F32, kind="ExternalOutput").ap()
    g_kin = nc.dram_tensor("g_kin", [HD, ROWS], BF16)
    g_vin = nc.dram_tensor("g_vin", [ROWS, 65], BF16)
    g_kout = nc.dram_tensor("g_kout", [NCORES * HD, ROWS], BF16)
    g_vout = nc.dram_tensor("g_vout", [S, 65], BF16)

    st = contextlib.ExitStack()

    def sbt(name, shape, dt):
        return st.enter_context(nc.sbuf_tensor(name, shape, dt))

    W1T = sbt("W1T", [128, HB, 4, 2, 128], FP8)
    xnT8 = sbt("xnT8", [128, 4, 4, 2, 512], FP8)
    wqb = sbt("wqb", [128, DB, HD], BF16)
    wkb = sbt("wkb", [128, DB, HD], BF16)
    wv1b = sbt("wv1b", [128, DB, HD], BF16)
    wv2b = sbt("wv2b", [HD, D], BF16)
    b1f8 = sbt("b1f8", [128, HB], FP8)
    b1q = sbt("b1q", [128, HB], F32)
    b2q8 = sbt("b2q8", [1, D], FP8)
    ones8 = sbt("ones8", [1, 128], FP8)
    onesbf = sbt("onesbf", [1, 1], BF16)
    identf = sbt("identf", [128, 128], F32)
    QT2 = sbt("QT2", [128, ROWS], BF16)
    Obuf = sbt("Obuf", [65, ROWS], BF16)
    lrow = sbt("lrow", [1, ROWS], BF16)
    rcols = sbt("rcols", [128, TPC], F32)
    stat = sbt("stat", [128, 32], F32)

    PA = st.enter_context(nc.psum_tensor("PA", [128, 1024], F32))
    PB = st.enter_context(nc.psum_tensor("PB", [128, 1024], F32))
    PC = st.enter_context(nc.psum_tensor("PC", [128, 2048], F32))

    sWdma = Sem(st, nc, "sWdma")
    sXd = [Sem(st, nc, f"sXd{i}") for i in range(6)]
    sSm = Sem(st, nc, "sSm")
    sW1b = [Sem(st, nc, f"sW1b{i}") for i in range(4)]
    sW2 = Sem(st, nc, "sW2")
    sPrep = Sem(st, nc, "sPrep")
    sMn = Sem(st, nc, "sMn")
    sSq = Sem(st, nc, "sSq")
    sVr = Sem(st, nc, "sVr")
    sRst = Sem(st, nc, "sRst")
    sXn = Sem(st, nc, "sXn")
    sTr = Sem(st, nc, "sTr")
    sEvac = Sem(st, nc, "sEvac")
    sEvV = Sem(st, nc, "sEvV")
    sProj = Sem(st, nc, "sProj")
    sPev = Sem(st, nc, "sPev")
    sGin = Sem(st, nc, "sGin")
    sCC = Sem(st, nc, "sCC")
    sGK = [Sem(st, nc, f"sGK{r}") for r in range(NCORES)]
    sGV = [Sem(st, nc, f"sGV{r}") for r in range(NCORES)]
    sS = Sem(st, nc, "sS")
    sE = Sem(st, nc, "sE")
    sMk = Sem(st, nc, "sMk")
    sAV = Sem(st, nc, "sAV")
    sOev = Sem(st, nc, "sOev")
    sL = Sem(st, nc, "sL")
    sRc = Sem(st, nc, "sRc")
    sH = Sem(st, nc, "sH")
    sHe = Sem(st, nc, "sHe")
    sH8 = Sem(st, nc, "sH8")
    sF2 = Sem(st, nc, "sF2")
    sYe = Sem(st, nc, "sYe")
    sA3 = Sem(st, nc, "sA3")
    sCb = Sem(st, nc, "sCb")
    sHeV = Sem(st, nc, "sHeV")
    sH8V = Sem(st, nc, "sH8V")
    sDm2 = [Sem(st, nc, f"sDm{i}") for i in range(2)]
    sOut2 = [Sem(st, nc, f"sOut{i}") for i in range(2)]

    block = st.enter_context(nc.Block())


    ga2 = contextlib.ExitStack()

    def sba2(name, shape, dt):
        return ga2.enter_context(nc.sbuf_tensor(name, shape, dt))

    KTall2 = sba2("KTall2", [128, S // 2], BF16)
    Vall = sba2("Vall", [128, NCH, 65], BF16)
    Pt4 = [sba2(f"Pt{i}", [128, 2048], BF16) for i in range(4)]
    maskc = sba2("maskcs", [128, 8, 128], BF16)

    ga1 = contextlib.ExitStack()

    def sba(name, shape, dt):
        return ga1.enter_context(nc.sbuf_tensor(name, shape, dt))

    xn_sc = [sba(f"xnsc{i}", [128, D], F32) for i in range(4)]
    xnTt = [sba(f"xnTt{i}", [128, DB, 128], BF16) for i in range(2)]
    KTloc = sba("KTloc", [HD, ROWS], BF16)
    Vloc = sba("Vloc", [128, TPC, 65], BF16)
    xbuf = [sba(f"xbuf{i}", [128, D], F32) for i in range(6)]

    @block.sync
    def _(sync):
        sWdma.inc(sync.dma_start(out=identf[:, :], in_=d_id[:, :]), 16)
        sWdma.inc(sync.dma_start(out=maskc[:, :, :], in_=d_mask[:, :, :]), 16)
        for t in range(32):
            if t >= 6:
                sync.wait_ge(sXn.h, t - 5)
            src = d_xprod if t < TPC else d_xown
            r0 = (t % TPC) * 128
            sXd[t % 6].inc(sync.dma_start(out=xbuf[t % 6][:, :], in_=src[r0:r0 + 128, :]), 16)

    @block.gpsimd
    def _(gpsimd):
        sSm.inc(gpsimd.dma_start(
            out=wqb[:, :, :], in_=d_wq.rearrange("(db p) k -> p db k", p=128)), 16)
        sSm.inc(gpsimd.dma_start(
            out=wkb[:, :, :], in_=d_wk.rearrange("(db p) k -> p db k", p=128)), 16)
        sSm.inc(gpsimd.dma_start(
            out=wv1b[:, :, :], in_=d_wv1.rearrange("(db p) k -> p db k", p=128)), 16)
        sSm.inc(gpsimd.dma_start(out=wv2b[:, :], in_=d_wv2[:, :]), 16)
        sSm.inc(gpsimd.dma_start(out=b1f8[:, :], in_=d_b1[:, :]), 16)
        sSm.inc(gpsimd.dma_start(out=b2q8[:, :], in_=d_b2[:, :]), 16)
        for bq in range(2):
            sW1b[bq].inc(gpsimd.dma_start(
                out=W1T[:, 8 * bq:8 * (bq + 1), :, :, :].rearrange(
                    "p a b c d -> p (a b c d)"),
                in_=d_w1t[:, bq * 8192:(bq + 1) * 8192]), 16)
        gpsimd.wait_ge(sPev.h, 3)
        sGin.inc(gpsimd.dma_start(out=g_kin.ap(), in_=KTloc[:, :]), 16)
        sGin.inc(gpsimd.dma_start(out=g_vin.ap().rearrange("(t p) e -> p t e", p=128),
                                  in_=Vloc[:, :, :]), 16)
        gpsimd.wait_ge(sGin.h, 32)
        sCC.inc(gpsimd.collective_compute(
            "AllGather", ALU.bypass, replica_groups=[list(range(NCORES))],
            ins=[g_kin.ap().opt()], outs=[g_kout.ap().opt()]))
        gpsimd.wait_ge(sCC.h, 1)
        for r in range(NCORES):
            srck = g_kout.ap()[r * HD:(r + 1) * HD, :].rearrange(
                "k (v two m) -> k v two m", two=2, m=128)
            sGK[r].inc(gpsimd.dma_start(
                out=KTall2[0:HD, r * 1024:(r + 1) * 1024].rearrange(
                    "k (v m) -> k v m", m=128),
                in_=srck[:, :, 0, :]), 16)
            sGK[r].inc(gpsimd.dma_start(
                out=KTall2[HD:128, r * 1024:(r + 1) * 1024].rearrange(
                    "k (v m) -> k v m", m=128),
                in_=srck[:, :, 1, :]), 16)
        sCC.inc(gpsimd.collective_compute(
            "AllGather", ALU.bypass, replica_groups=[list(range(NCORES))],
            ins=[g_vin.ap().opt()], outs=[g_vout.ap().opt()]))
        gpsimd.wait_ge(sCC.h, 2)
        for r in range(NCORES):
            sGV[r].inc(gpsimd.dma_start(
                out=Vall[:, r * TPC:(r + 1) * TPC, :],
                in_=g_vout.ap()[r * ROWS:(r + 1) * ROWS, :].rearrange(
                    "(g p) e -> p g e", p=128)), 16)
        for bq in range(2, 4):
            sW1b[bq].inc(gpsimd.dma_start(
                out=W1T[:, 8 * bq:8 * (bq + 1), :, :, :].rearrange(
                    "p a b c d -> p (a b c d)"),
                in_=d_w1t[:, bq * 8192:(bq + 1) * 8192]), 16)

    @block.vector
    def _(vector):
        vector.wait_ge(sSm.h, 16 * 6)
        vector.tensor_copy(b1q[:, :], b1f8[:, :])
        vector.memset(ones8[0:1, :], 1.0)
        vector.memset(onesbf[0:1, 0:1], 1.0)
        sPrep.inc(vector.memset(Vloc[:, :, 64:65], 1.0))

        def front(t):
            B = 8 * (t % 4)
            vector.wait_ge(sXd[t % 6].h, 16 * (t // 6 + 1))
            if t >= 4:
                vector.wait_ge(sRst.h, t - 3)
            if t >= 4:
                vector.wait_ge(sTr.h, t - 3)
            sMn.inc(vector.tensor_scalar(xn_sc[t % 4][:, :], xbuf[t % 6][:, :],
                                         -1.0 / D, None, ALU.mult, ALU.add,
                                         accum_out=stat[:, B + 2:B + 3]))

        def mid(u):
            B = 8 * (u % 4)
            vector.wait_ge(sSq.h, u + 1)
            vector.tensor_scalar(stat[:, B + 5:B + 6], stat[:, B + 1:B + 2],
                                 1.0 / D, 1e-5, ALU.mult, ALU.add)
            vector.drain()
            sVr.inc(vector.reciprocal(stat[:, B + 6:B + 7], stat[:, B + 5:B + 6]))

        def xnst(u):
            B = 8 * (u % 4)
            vector.wait_ge(sRst.h, u + 1)
            sXn.inc(vector.tensor_scalar(xn_sc[u % 4][:, :], xbuf[u % 6][:, :],
                                         stat[:, B + 2:B + 3], stat[:, B + 7:B + 8],
                                         ALU.add, ALU.mult))

        def evac(u):
            own = u >= TPC
            tl = u % TPC
            vector.wait_ge(sEvac.h, u + 1)
            if own:
                TRB = PA[:, 0:1024] if u % 2 == 0 else PC[:, 1024:2048]
                wq_ = tl // 4
                tp = (tl % 4) * 128
                sEvV.inc(vector.tensor_copy(
                    xnT8[:, wq_, :, :, tp:tp + 128],
                    TRB[:, :].rearrange("p (j i c) -> p j i c", j=4, i=2)))
            if u == 7:
                vector.wait_ge(sProj.h, 8)
                sPev.inc(vector.tensor_copy(Vloc[:, 0:8, 0:64],
                                            PB[:, 0:512].rearrange("p (g e) -> p g e", g=8)))
            if u == 15:
                vector.wait_ge(sProj.h, 16)
                sPev.inc(vector.tensor_copy(Vloc[:, 8:16, 0:64],
                                            PB[:, 0:512].rearrange("p (g e) -> p g e", g=8)))
                vector.tensor_copy(KTloc[:, 0:1024], PC[0:HD, 0:1024])
                sPev.inc(vector.tensor_copy(KTloc[:, 1024:2048], PC[64:128, 0:1024]))

        for t in range(32):
            if t >= 2:
                xnst(t - 2)
            if t >= 3:
                evac(t - 3)
            front(t)
            if t >= 1:
                mid(t - 1)
        xnst(30)
        evac(29)
        mid(31)
        xnst(31)
        evac(30)
        evac(31)
        vector.wait_ge(sProj.h, 32)
        vector.tensor_copy(QT2[0:HD, 0:1024], PC[0:HD, 0:1024])
        vector.tensor_copy(QT2[0:HD, 1024:2048], PC[64:128, 0:1024])
        vector.tensor_copy(QT2[HD:128, 0:1024], PC[0:HD, 0:1024])
        sPev.inc(vector.tensor_copy(QT2[HD:128, 1024:2048], PC[64:128, 0:1024]))

    @block.scalar
    def _(scalar):
        def sq(t):
            B = 8 * (t % 4)
            scalar.wait_ge(sMn.h, t + 1)
            sSq.inc(scalar.activation(xn_sc[t % 4][:, :], xbuf[t % 6][:, :], AF.Square,
                                      bias=stat[:, B + 2:B + 3],
                                      accum_out=stat[:, B + 1:B + 2]))

        def rt(u):
            B = 8 * (u % 4)
            scalar.wait_ge(sVr.h, u + 1)
            sRst.inc(scalar.activation(stat[:, B + 7:B + 8], stat[:, B + 6:B + 7], AF.Sqrt))

        def evs(u):
            TRB = PA[:, 0:1024] if u % 2 == 0 else PC[:, 1024:2048]
            scalar.wait_ge(sTr.h, u + 1)
            if u >= 2:
                scalar.wait_ge(sProj.h, u - 1)
            scalar.activation(
                xnTt[u % 2][:, 0:4, :].rearrange("p a b -> p (a b)"),
                TRB[:, 0:512], AF.Identity)
            sEvac.inc(scalar.activation(
                xnTt[u % 2][:, 4:8, :].rearrange("p a b -> p (a b)"),
                TRB[:, 512:1024], AF.Identity))

        for t in range(32):
            sq(t)
            if t >= 1:
                rt(t - 1)
            if t >= 2:
                evs(t - 2)
        rt(31)
        evs(30)
        evs(31)

    @block.tensor
    def _(tensor):
        tensor.wait_ge(sPrep.h, 1)
        tensor.wait_ge(sWdma.h, 32)
        for t in range(32):
            own = t >= TPC
            tl = t % TPC
            TRB = PA[:, 0:1024] if t % 2 == 0 else PC[:, 1024:2048]
            tensor.wait_ge(sXn.h, t + 1)
            if t >= 2:
                tensor.wait_ge(sEvac.h, t - 1)
                if t - 2 >= TPC:
                    tensor.wait_ge(sEvV.h, t - 1 - TPC)
            for db in range(DB):
                ins = tensor.transpose(
                    TRB[:, db * 128:(db + 1) * 128],
                    xn_sc[t % 4][:, db * 128:(db + 1) * 128], identf[:, :])
                if db == DB - 1:
                    sTr.inc(ins)
            tensor.wait_ge(sEvac.h, t + 1)
            half = (tl % TPC) >= 8
            pdst = PC[64:128, (tl % 8) * 128:(tl % 8) * 128 + 128] if half \
                else PC[0:HD, tl * 128:(tl + 1) * 128]
            tpos = (0, 64) if half else None
            if not own:
                if t == 8:
                    tensor.wait_ge(sPev.h, 1)
                for db in range(DB):
                    tensor.matmul(pdst, wkb[:, db, :], xnTt[t % 2][:, db, :],
                                  start=(db == 0), stop=(db == DB - 1),
                                  tile_position=tpos)
                for db in range(DB):
                    ins = tensor.matmul(PB[:, (tl % 8) * 64:(tl % 8) * 64 + 64],
                                        xnTt[t % 2][:, db, :], wv1b[:, db, :],
                                        start=(db == 0), stop=(db == DB - 1))
                    if db == DB - 1:
                        sProj.inc(ins)
            else:
                if t == TPC:
                    tensor.wait_ge(sPev.h, 3)
                for db in range(DB):
                    ins = tensor.matmul(pdst, wqb[:, db, :], xnTt[t % 2][:, db, :],
                                        start=(db == 0), stop=(db == DB - 1),
                                        tile_position=tpos)
                    if db == DB - 1:
                        sProj.inc(ins)

    PEV_ALL = sPev.n
    WDMA_ALL = sWdma.n
    EV_ALL = sEvac.n

    ga1.close()

    gh = contextlib.ExitStack()

    def sbh(name, shape, dt):
        return gh.enter_context(nc.sbuf_tensor(name, shape, dt))

    W2T = sbh("W2T", [128, 16, 2, 2, 512], FP8)
    h8w = sbh("h8w", [128, 2, 16, 4, 2, 128], FP8)
    hf16 = sbh("hf16", [128, 2, 2, 512], FP16)
    NG2 = 64
    SLOT = [0, 1, 0, 1]

    @block.gpsimd
    def _(gpsimd):
        gpsimd.wait_ge(sSq.h, 32)
        gpsimd.wait_ge(sXn.h, 32)
        gpsimd.wait_ge(sTr.h, 32)
        gpsimd.wait_ge(sEvac.h, 32)
        gpsimd.wait_ge(sEvV.h, TPC)
        gpsimd.wait_ge(sProj.h, 32)
        gpsimd.wait_ge(sPev.h, 4)
        gpsimd.wait_ge(sGin.h, 32)
        for q in range(4):
            sW2.inc(gpsimd.dma_start(
                out=W2T[:, 4 * q:4 * (q + 1), :, :, :].rearrange(
                    "p a b c d -> p (a b c d)"),
                in_=d_w2t[:, q * 8192:(q + 1) * 8192]), 16)

    piters = []
    itcum = [0] * 66
    for q in range(64):
        lo0 = 128 * ((2 * q) >> 3)
        for (a, b) in _pieces(lo0, 2048):
            piters.append((q, a, b, a == lo0))
        itcum[q + 1] = len(piters)
    NIT = len(piters)
    OEV_WAIT = [itcum[16 * j + 15 + 1] for j in range(4)]

    @block.tensor
    def _(tensor):
        tensor.wait_ge(sPev.h, PEV_ALL)
        tensor.wait_ge(sEvac.h, EV_ALL)
        tensor.wait_ge(sEvV.h, TPC)
        B3 = [PA[:, 0:512], PA[:, 512:1024], PB[:, 0:512]]
        psH = PB[:, 512:1024]
        gq = 0
        seenK = set()
        seenV = set()

        def ffn1_p2(due):
            nonlocal gq
            while gq < due:
                w, hb = gq // HB, gq % HB
                tensor.wait_ge(sW1b[hb // 8].h, 16)
                if gq >= 1:
                    tensor.wait_ge(sHeV.h, gq)
                for j in range(4):
                    ins = tensor.matmul(psH, W1T[:, hb, j, :, :], xnT8[:, w, j, :, :],
                                        start=(j == 0), stop=(j == 3),
                                        perf_mode=mybir.MatmulPerfMode.DoubleRow)
                sH.inc(ins)
                gq += 1

        ffn1_p2(8)
        for it, (q, a, b, diag) in enumerate(piters):
            if q // 8 not in seenK:
                tensor.wait_ge(sGK[q // 8].h, 32)
                seenK.add(q // 8)
            for eo in range(2):
                s = 2 * it + eo
                if s >= 3:
                    tensor.wait_ge(sE.h, s - 2)
                sS.inc(tensor.matmul(
                    B3[s % 3][:, 0:b - a],
                    KTall2[eo * HD:eo * HD + HD, 128 * q:128 * (q + 1)],
                    QT2[eo * HD:eo * HD + HD, a:b], start=True, stop=True))
            for av_it in ([it - 1] if it >= 1 else []) + ([NIT - 1] if it == NIT - 1 else []):
                (q2, a2, b2, diag2) = piters[av_it]
                if q2 // 8 not in seenV:
                    tensor.wait_ge(sGV[q2 // 8].h, 16)
                    seenV.add(q2 // 8)
                tensor.wait_ge(sE.h, 2 * (av_it + 1))
                if diag2:
                    tensor.wait_ge(sMk.h, q2 + 1)
                for eo in range(2):
                    stop = (eo == 1 and q2 == 16 * (a2 // 512) + 15)
                    ins = tensor.matmul(
                        PC[0:65, a2:b2], Vall[:, 2 * q2 + eo, :],
                        Pt4[2 * (q2 % 2) + eo][:, a2:b2],
                        start=(q2 == 0 and eo == 0), stop=stop)
                sAV.inc(ins)
            ffn1_p2(min(NG2, ((it + 1) * NG2) // (NIT - 1)))

    @block.scalar
    def _(scalar):
        B3 = [PA[:, 0:512], PA[:, 512:1024], PB[:, 0:512]]
        for it, (q, a, b, diag) in enumerate(piters):
            if q >= 2:
                scalar.wait_ge(sAV.h, itcum[q - 1])
            for eo in range(2):
                s = 2 * it + eo
                scalar.wait_ge(sS.h, s + 1)
                sE.inc(scalar.activation(
                    Pt4[2 * (q % 2) + eo][:, a:b], B3[s % 3][:, 0:b - a],
                    AF.Exp, bias=0.0, scale=float(HD) ** -0.5))

    @block.vector
    def _(vector):
        vector.wait_ge(sWdma.h, WDMA_ALL)
        gv = 0

        def emit_tasks(due):
            nonlocal gv
            while gv < due:
                w, hb = gv // HB, gv % HB
                vector.wait_ge(sH.h, gv + 1)
                sHeV.inc(vector.tensor_scalar(
                    hf16[:, (hb // 2) % 2, hb % 2, :], PB[:, 512:1024],
                    b1q[:, hb:hb + 1], 0.0, ALU.add, ALU.max))
                if hb % 2 == 1:
                    j = hb // 2
                    vector.drain()
                    for i in range(2):
                        sH8V.inc(vector.tensor_copy(
                            h8w[:, SLOT[w], j, :, i, :],
                            hf16[:, j % 2, i, :].rearrange("p (tl m) -> p tl m", m=128)))
                gv += 1

        emit_tasks(8)
        for it, (q, a, b, diag) in enumerate(piters):
            if diag:
                m = (2 * q) >> 3
                vector.wait_ge(sE.h, 2 * (it + 1))
                vector.tensor_tensor(
                    Pt4[2 * (q % 2)][:, 128 * m:128 * m + 128],
                    Pt4[2 * (q % 2)][:, 128 * m:128 * m + 128],
                    maskc[:, (2 * q) & 7, :], ALU.mult)
                sMk.inc(vector.tensor_tensor(
                    Pt4[2 * (q % 2) + 1][:, 128 * m:128 * m + 128],
                    Pt4[2 * (q % 2) + 1][:, 128 * m:128 * m + 128],
                    maskc[:, (2 * q + 1) & 7, :], ALU.mult))
            emit_tasks(min(NG2, (it * NG2) // (NIT - 1)))
        emit_tasks(NG2)
        for j in range(4):
            vector.wait_ge(sAV.h, OEV_WAIT[j])
            vector.tensor_copy(lrow[0:1, 512 * j:512 * (j + 1)],
                               PC[64:65, 512 * j:512 * (j + 1)])
            sOev.inc(vector.tensor_copy(Obuf[:, 512 * j:512 * (j + 1)],
                                        PC[0:65, 512 * j:512 * (j + 1)]))

    gb = contextlib.ExitStack()

    def sbb(name, shape, dt):
        return gb.enter_context(nc.sbuf_tensor(name, shape, dt))

    yf16 = [sbb(f"yf16{i}", [128, 512], FP16) for i in range(2)]
    dmbuf = [sbb(f"dmbuf{i}", [128, D], BF16) for i in range(2)]
    cmb = [sbb(f"cmb{i}", [128, D], F32) for i in range(2)]
    tmp2 = [sbb(f"tmp2{i}", [128, 512], F32) for i in range(2)]

    @block.tensor
    def _(tensor):
        tensor.wait_ge(sE.h, 2 * NIT)
        tensor.wait_ge(sHeV.h, NG2)
        tensor.wait_ge(sOev.h, 4)
        for t in range(TPC):
            ins = tensor.matmul(PA[:, t:t + 1], lrow[0:1, t * 128:(t + 1) * 128],
                                onesbf[0:1, 0:1], start=True, stop=True)
            if t == TPC - 1:
                sL.inc(ins)

        psH3 = [PA[:, 0:512], PA[:, 512:1024], PC[:, 1536:2048]]
        psY3 = [PB[:, 0:512], PB[:, 512:1024], PC[:, 1024:1536]]
        g3 = 0

        def ffn1_groups(tensor, k):
            nonlocal g3
            for _ in range(k):
                if g3 >= 2 * HB:
                    return
                wv, hb = 2 + g3 // HB, g3 % HB
                if g3 < 2:
                    tensor.wait_ge(sRc.h, 1)
                elif g3 >= 3:
                    tensor.wait_ge(sHe.h, g3 - 2)
                for j in range(4):
                    ins = tensor.matmul(
                        psH3[g3 % 3], W1T[:, hb, j, :, :], xnT8[:, wv, j, :, :],
                        start=(j == 0), stop=(j == 3),
                        perf_mode=mybir.MatmulPerfMode.DoubleRow)
                sH.inc(ins)
                g3 += 1

        for w in range(4):
            for tl in range(4):
                tg = 4 * w + tl
                for nb in range(2):
                    n = 2 * tg + nb
                    if n >= 8:
                        ffn1_groups(tensor, 4)
                    if w < 2:
                        tensor.wait_ge(sH8V.h, 32 * (w + 1))
                    else:
                        tensor.wait_ge(sH8.h, 32 * (w - 1))
                    if n == 0:
                        tensor.wait_ge(sW2.h, 64)
                    if n >= 3:
                        tensor.wait_ge(sYe.h, n - 2)
                    for j in range(16):
                        tensor.matmul(psY3[n % 3],
                                      h8w[:, SLOT[w], j, tl, :, :],
                                      W2T[:, j, nb, :, :],
                                      start=(j == 0), stop=False,
                                      perf_mode=mybir.MatmulPerfMode.DoubleRow)
                    sF2.inc(tensor.matmul(psY3[n % 3],
                                          ones8[0:1, 0:128],
                                          b2q8[0:1, nb * 512:(nb + 1) * 512],
                                          start=False, stop=True))
                    if n >= 2:
                        tensor.wait_ge(sCb.h, n - 1)
                    sA3.inc(tensor.matmul(PC[:, (n % 2) * 512:(n % 2) * 512 + 512],
                                          Obuf[0:HD, tg * 128:(tg + 1) * 128],
                                          wv2b[0:HD, nb * 512:(nb + 1) * 512],
                                          start=True, stop=True))

    @block.vector
    def _(vector):
        vector.wait_ge(sL.h, 1)
        sRc.inc(vector.reciprocal(rcols[:, :], PA[:, 0:TPC]))
        vector.drain()
        j3 = 0
        for w in range(4):
            for tl in range(4):
                tg = 4 * w + tl
                for nb in range(2):
                    n = 2 * tg + nb
                    for _ in range(2 if n >= 9 else 0):
                        if j3 < 32:
                            vector.wait_ge(sHe.h, 2 * j3 + 2)
                            if j3 % 16 == 0:
                                vector.wait_ge(sF2.h, 8 * (j3 // 16 + 1))
                            for i in range(2):
                                sH8.inc(vector.tensor_copy(
                                    h8w[:, j3 // 16, j3 % 16, :, i, :],
                                    hf16[:, j3 % 2, i, :].rearrange(
                                        "p (tl m) -> p tl m", m=128)))
                            j3 += 1
                    vector.wait_ge(sA3.h, n + 1)
                    vector.wait_ge(sYe.h, n + 1)
                    vector.wait_ge(sDm2[tg % 2].h, 32 * (tg // 2 + 1))
                    if tg >= 2:
                        vector.wait_ge(sOut2[tg % 2].h, 16 * (tg // 2))
                    vector.scalar_tensor_tensor(
                        cmb[tg % 2][:, nb * 512:(nb + 1) * 512],
                        PC[:, (n % 2) * 512:(n % 2) * 512 + 512],
                        rcols[:, tg:tg + 1],
                        cmb[tg % 2][:, nb * 512:(nb + 1) * 512],
                        ALU.mult, ALU.add)
                    vector.tensor_tensor(tmp2[n % 2][:, :], yf16[n % 2][:, :],
                                         dmbuf[tg % 2][:, nb * 512:(nb + 1) * 512],
                                         ALU.mult)
                    vector.drain()
                    sCb.inc(vector.tensor_tensor(
                        cmb[tg % 2][:, nb * 512:(nb + 1) * 512],
                        cmb[tg % 2][:, nb * 512:(nb + 1) * 512],
                        tmp2[n % 2][:, :], ALU.add))

    @block.scalar
    def _(scalar):
        scalar.wait_ge(sH8V.h, NG2)
        psH3 = [PA[:, 0:512], PA[:, 512:1024], PC[:, 1536:2048]]
        psY3 = [PB[:, 0:512], PB[:, 512:1024], PC[:, 1024:1536]]
        g3 = 0
        for w in range(4):
            for tl in range(4):
                for nb in range(2):
                    n = 2 * (4 * w + tl) + nb
                    for _ in range(4 if n >= 8 else 0):
                        if g3 < 2 * HB:
                            scalar.wait_ge(sH.h, NG2 + g3 + 1)
                            if g3 >= 4:
                                scalar.wait_ge(sH8.h, 2 * (g3 // 2) - 2)
                            sHe.inc(scalar.activation(
                                hf16[:, (g3 // 2) % 2, g3 % 2, :],
                                psH3[g3 % 3],
                                AF.Relu, bias=b1q[:, (g3 % HB):(g3 % HB) + 1]))
                            g3 += 1
                    scalar.wait_ge(sF2.h, n + 1)
                    if n >= 2:
                        scalar.wait_ge(sCb.h, n - 1)
                    sYe.inc(scalar.activation(yf16[n % 2][:, :],
                                              psY3[n % 3],
                                              AF.Identity))

    @block.gpsimd
    def _(gpsimd):
        for tg in range(TPC):
            if tg >= 2:
                gpsimd.wait_ge(sCb.h, 2 * (tg - 1))
                gpsimd.wait_ge(sOut2[tg % 2].h, 16 * (tg // 2))
            sDm2[tg % 2].inc(gpsimd.dma_start(out=dmbuf[tg % 2][:, :],
                                     in_=d_dm[tg * 128:(tg + 1) * 128, :]), 16)
            sDm2[tg % 2].inc(gpsimd.dma_start(out=cmb[tg % 2][:, :],
                                     in_=d_xown[tg * 128:(tg + 1) * 128, :]), 16)

    @block.sync
    def _(sync):
        for tg in range(TPC):
            sync.wait_ge(sCb.h, 2 * (tg + 1))
            sOut2[tg % 2].inc(sync.dma_start(out=d_out[tg * 128:(tg + 1) * 128, :],
                                             in_=cmb[tg % 2][:, :]), 16)
        sync.wait_ge(sOut2[0].h, 16 * (TPC // 2))
        sync.wait_ge(sOut2[1].h, 16 * (TPC // 2))

    gb.close()
    gh.close()
    ga2.close()
    return nc


_NC_CACHE = None


def _get_nc():
    global _NC_CACHE
    if _NC_CACHE is None:
        _NC_CACHE = build_nc()
    return _NC_CACHE


def kernel(L0_attn_x, L0_attn_w_q, L0_attn_w_k, L0_attn_w_v1, L0_attn_w_v2,
           L0_ffn_w1, L0_ffn_b1, L0_ffn_w2, L0_ffn_b2):
    x = np.ascontiguousarray(np.asarray(L0_attn_x, dtype=np.float32).reshape(S, D))
    wq = np.ascontiguousarray(np.asarray(L0_attn_w_q, dtype=np.float32))
    wk = np.ascontiguousarray(np.asarray(L0_attn_w_k, dtype=np.float32))
    wv1 = np.ascontiguousarray(np.asarray(L0_attn_w_v1, dtype=np.float32))
    wv2 = np.ascontiguousarray(np.asarray(L0_attn_w_v2, dtype=np.float32))
    w1 = np.ascontiguousarray(np.asarray(L0_ffn_w1, dtype=np.float32))
    b1 = np.asarray(L0_ffn_b1, dtype=np.float32)
    w2 = np.ascontiguousarray(np.asarray(L0_ffn_w2, dtype=np.float32))
    b2 = np.asarray(L0_ffn_b2, dtype=np.float32)

    w1dr = np.ascontiguousarray(
        w1.reshape(HB, 128, 4, 2, 128).transpose(4, 0, 2, 3, 1).reshape(128, -1))
    w2dr = np.ascontiguousarray(
        w2.reshape(2, 512, 16, 2, 128).transpose(4, 2, 0, 3, 1).reshape(128, -1))
    b1r = np.ascontiguousarray(b1.reshape(HB, 128).T)
    b2r = np.ascontiguousarray(b2.reshape(1, D))
    ident = np.eye(128, dtype=np.float32)

    import jax
    with jax.default_device(jax.devices("cpu")[0]):
        keep = np.asarray(jax.random.bernoulli(
            jax.random.key(42, impl="threefry2x32"), 0.8, (1, S, D)))
    dmf = np.where(keep.reshape(S, D), np.float32(1.25), np.float32(0.0))
    dm_all = dmf.astype(ml_dtypes.bfloat16)

    x_tiles = x.reshape(S // 128, 128, D)
    dm_tiles = dm_all.reshape(S // 128, 128, D)

    in_maps = []
    for c in range(NCORES):
        own = [8 * t + c for t in range(TPC)]
        p_idx = np.arange(128)[:, None, None]
        r_idx = np.arange(8)[None, :, None]
        j_idx = np.arange(128)[None, None, :]
        mc = ((128 * r_idx + p_idx - 128 * c) <= j_idx).astype(ml_dtypes.bfloat16)
        in_maps.append({
            "x_prod": np.ascontiguousarray(x[ROWS * c:ROWS * (c + 1)]),
            "x_own": np.ascontiguousarray(x_tiles[own].reshape(ROWS, D)),
            "wq": wq, "wk": wk, "wv1": wv1, "wv2": wv2,
            "w1t": w1dr, "b1r": b1r,
            "w2t": w2dr, "b2r": b2r,
            "maskc": np.ascontiguousarray(mc),
            "dm": np.ascontiguousarray(dm_tiles[own].reshape(ROWS, D)),
            "ident": ident,
        })

    nc = _get_nc()
    res = run_bass_kernel_spmd(nc, in_maps, core_ids=list(range(NCORES)))

    out = np.empty((S // 128, 128, D), dtype=np.float32)
    for c in range(NCORES):
        own = [8 * t + c for t in range(TPC)]
        out[own] = np.asarray(res.results[c]["out"], dtype=np.float32).reshape(TPC, 128, D)
    return out.reshape(1, S, D)


# version 18
# speedup vs baseline: 1.1596x; 1.1596x over previous
import contextlib
import numpy as np
import ml_dtypes

import concourse.bass as bass
import concourse.mybir as mybir
from concourse.bass_utils import run_bass_kernel_spmd

F32 = mybir.dt.float32
BF16 = mybir.dt.bfloat16
FP16 = mybir.dt.float16
FP8 = mybir.dt.float8e4
AF = mybir.ActivationFunctionType
ALU = mybir.AluOpType

S, D, HD, H = 16384, 1024, 64, 4096
NCORES = 8
TPC = 16
ROWS = TPC * 128
NCH = S // 128
DB = D // 128
HB = H // 128


def _pieces(lo, hi):
    out = []
    a = lo
    while a < hi:
        b = min((a // 512 + 1) * 512, hi)
        out.append((a, b))
        a = b
    return out


def _halves(kp):
    lo = 128 * (kp >> 3)
    if lo < 1024:
        return [(lo, 1024, 'A', 0), (1024, 2048, 'B', 1024)]
    return [(lo, 2048, 'A' if (kp & 1) else 'B', 1024)]


class Sem:
    def __init__(self, stack, nc, name):
        self.h = stack.enter_context(nc.semaphore(name))
        self.n = 0

    def inc(self, instr, k=1):
        instr.then_inc(self.h, k)
        self.n += k
        return instr


def build_nc():
    nc = bass.Bass("TRN2", target_bir_lowering=False)

    d_xprod = nc.dram_tensor("x_prod", [ROWS, D], F32, kind="ExternalInput").ap()
    d_xown = nc.dram_tensor("x_own", [ROWS, D], F32, kind="ExternalInput").ap()
    d_wq = nc.dram_tensor("wq", [D, HD], F32, kind="ExternalInput").ap()
    d_wk = nc.dram_tensor("wk", [D, HD], F32, kind="ExternalInput").ap()
    d_wv1 = nc.dram_tensor("wv1", [D, HD], F32, kind="ExternalInput").ap()
    d_wv2 = nc.dram_tensor("wv2", [HD, D], F32, kind="ExternalInput").ap()
    d_w1t = nc.dram_tensor("w1t", [128, D * H // 128], F32, kind="ExternalInput").ap()
    d_b1 = nc.dram_tensor("b1r", [128, HB], F32, kind="ExternalInput").ap()
    d_w2t = nc.dram_tensor("w2t", [128, D * H // 128], F32, kind="ExternalInput").ap()
    d_b2 = nc.dram_tensor("b2r", [1, D], F32, kind="ExternalInput").ap()
    d_mask = nc.dram_tensor("maskc", [128, 8, 128], BF16, kind="ExternalInput").ap()
    d_dm = nc.dram_tensor("dm", [ROWS, D], BF16, kind="ExternalInput").ap()
    d_id = nc.dram_tensor("ident", [128, 128], F32, kind="ExternalInput").ap()
    d_out = nc.dram_tensor("out", [ROWS, D], F32, kind="ExternalOutput").ap()
    g_kin = nc.dram_tensor("g_kin", [HD, ROWS], BF16)
    g_vin = nc.dram_tensor("g_vin", [ROWS, 65], BF16)
    g_kout = nc.dram_tensor("g_kout", [NCORES * HD, ROWS], BF16)
    g_vout = nc.dram_tensor("g_vout", [S, 65], BF16)

    st = contextlib.ExitStack()

    def sbt(name, shape, dt):
        return st.enter_context(nc.sbuf_tensor(name, shape, dt))

    W1T = sbt("W1T", [128, HB, 4, 2, 128], FP8)
    xnT8 = sbt("xnT8", [128, 4, 4, 2, 512], FP8)
    wqb = sbt("wqb", [128, DB, HD], BF16)
    wkb = sbt("wkb", [128, DB, HD], BF16)
    wv1b = sbt("wv1b", [128, DB, HD], BF16)
    wv2b = sbt("wv2b", [HD, D], BF16)
    b1f8 = sbt("b1f8", [128, HB], FP8)
    b1q = sbt("b1q", [128, HB], F32)
    b2q8 = sbt("b2q8", [1, D], FP8)
    ones8 = sbt("ones8", [1, 128], FP8)
    onesbf = sbt("onesbf", [1, 1], BF16)
    identf = sbt("identf", [128, 128], F32)
    QT2 = sbt("QT2", [128, ROWS], BF16)
    Obuf = sbt("Obuf", [65, ROWS], BF16)
    lrow = sbt("lrow", [1, ROWS], BF16)
    rcols = sbt("rcols", [128, TPC], F32)
    stat = sbt("stat", [128, 32], F32)

    PA = st.enter_context(nc.psum_tensor("PA", [128, 1024], F32))
    PB = st.enter_context(nc.psum_tensor("PB", [128, 1024], F32))
    PC = st.enter_context(nc.psum_tensor("PC", [128, 2048], F32))

    sWdma = Sem(st, nc, "sWdma")
    sXd = [Sem(st, nc, f"sXd{i}") for i in range(6)]
    sSm = Sem(st, nc, "sSm")
    sW1b = [Sem(st, nc, f"sW1b{i}") for i in range(4)]
    sW2 = Sem(st, nc, "sW2")
    sPrep = Sem(st, nc, "sPrep")
    sMn = Sem(st, nc, "sMn")
    sSq = Sem(st, nc, "sSq")
    sVr = Sem(st, nc, "sVr")
    sRst = Sem(st, nc, "sRst")
    sXn = Sem(st, nc, "sXn")
    sTr = Sem(st, nc, "sTr")
    sEvac = Sem(st, nc, "sEvac")
    sEvV = Sem(st, nc, "sEvV")
    sProj = Sem(st, nc, "sProj")
    sPev = Sem(st, nc, "sPev")
    sGin = Sem(st, nc, "sGin")
    sCC = Sem(st, nc, "sCC")
    sGK = [Sem(st, nc, f"sGK{r}") for r in range(NCORES)]
    sGV = [Sem(st, nc, f"sGV{r}") for r in range(NCORES)]
    sS = Sem(st, nc, "sS")
    sE = Sem(st, nc, "sE")
    sMk = Sem(st, nc, "sMk")
    sAV = Sem(st, nc, "sAV")
    sOev = Sem(st, nc, "sOev")
    sL = Sem(st, nc, "sL")
    sRc = Sem(st, nc, "sRc")
    sH = Sem(st, nc, "sH")
    sHe = Sem(st, nc, "sHe")
    sH8 = Sem(st, nc, "sH8")
    sF2 = Sem(st, nc, "sF2")
    sYe = Sem(st, nc, "sYe")
    sA3 = Sem(st, nc, "sA3")
    sCb = Sem(st, nc, "sCb")
    sHeV = Sem(st, nc, "sHeV")
    sH8V = Sem(st, nc, "sH8V")
    sDm2 = [Sem(st, nc, f"sDm{i}") for i in range(2)]
    sOut2 = [Sem(st, nc, f"sOut{i}") for i in range(2)]

    block = st.enter_context(nc.Block())


    ga2 = contextlib.ExitStack()

    def sba2(name, shape, dt):
        return ga2.enter_context(nc.sbuf_tensor(name, shape, dt))

    KTall2 = sba2("KTall2", [128, S // 2], BF16)
    Vall = sba2("Vall", [128, NCH, 65], BF16)
    Pt4 = [sba2(f"Pt{i}", [128, 2048], BF16) for i in range(4)]
    maskc = sba2("maskcs", [128, 8, 128], BF16)

    ga1 = contextlib.ExitStack()

    def sba(name, shape, dt):
        return ga1.enter_context(nc.sbuf_tensor(name, shape, dt))

    xn_sc = [sba(f"xnsc{i}", [128, D], F32) for i in range(4)]
    xnTt = [sba(f"xnTt{i}", [128, DB, 128], BF16) for i in range(2)]
    KTloc = sba("KTloc", [HD, ROWS], BF16)
    Vloc = sba("Vloc", [128, TPC, 65], BF16)
    xbuf = [sba(f"xbuf{i}", [128, D], F32) for i in range(6)]

    @block.sync
    def _(sync):
        sWdma.inc(sync.dma_start(out=identf[:, :], in_=d_id[:, :]), 16)
        sWdma.inc(sync.dma_start(out=maskc[:, :, :], in_=d_mask[:, :, :]), 16)
        for t in range(32):
            if t >= 6:
                sync.wait_ge(sXn.h, t - 5)
            src = d_xprod if t < TPC else d_xown
            r0 = (t % TPC) * 128
            sXd[t % 6].inc(sync.dma_start(out=xbuf[t % 6][:, :], in_=src[r0:r0 + 128, :]), 16)

    @block.gpsimd
    def _(gpsimd):
        sSm.inc(gpsimd.dma_start(
            out=wqb[:, :, :], in_=d_wq.rearrange("(db p) k -> p db k", p=128)), 16)
        sSm.inc(gpsimd.dma_start(
            out=wkb[:, :, :], in_=d_wk.rearrange("(db p) k -> p db k", p=128)), 16)
        sSm.inc(gpsimd.dma_start(
            out=wv1b[:, :, :], in_=d_wv1.rearrange("(db p) k -> p db k", p=128)), 16)
        sSm.inc(gpsimd.dma_start(out=wv2b[:, :], in_=d_wv2[:, :]), 16)
        sSm.inc(gpsimd.dma_start(out=b1f8[:, :], in_=d_b1[:, :]), 16)
        sSm.inc(gpsimd.dma_start(out=b2q8[:, :], in_=d_b2[:, :]), 16)
        for bq in range(2):
            sW1b[bq].inc(gpsimd.dma_start(
                out=W1T[:, 8 * bq:8 * (bq + 1), :, :, :].rearrange(
                    "p a b c d -> p (a b c d)"),
                in_=d_w1t[:, bq * 8192:(bq + 1) * 8192]), 16)
        gpsimd.wait_ge(sPev.h, 3)
        sGin.inc(gpsimd.dma_start(out=g_kin.ap(), in_=KTloc[:, :]), 16)
        sGin.inc(gpsimd.dma_start(out=g_vin.ap().rearrange("(t p) e -> p t e", p=128),
                                  in_=Vloc[:, :, :]), 16)
        gpsimd.wait_ge(sGin.h, 32)
        sCC.inc(gpsimd.collective_compute(
            "AllGather", ALU.bypass, replica_groups=[list(range(NCORES))],
            ins=[g_kin.ap().opt()], outs=[g_kout.ap().opt()]))
        gpsimd.wait_ge(sCC.h, 1)
        for r in range(NCORES):
            srck = g_kout.ap()[r * HD:(r + 1) * HD, :].rearrange(
                "k (v two m) -> k v two m", two=2, m=128)
            sGK[r].inc(gpsimd.dma_start(
                out=KTall2[0:HD, r * 1024:(r + 1) * 1024].rearrange(
                    "k (v m) -> k v m", m=128),
                in_=srck[:, :, 0, :]), 16)
            sGK[r].inc(gpsimd.dma_start(
                out=KTall2[HD:128, r * 1024:(r + 1) * 1024].rearrange(
                    "k (v m) -> k v m", m=128),
                in_=srck[:, :, 1, :]), 16)
        sCC.inc(gpsimd.collective_compute(
            "AllGather", ALU.bypass, replica_groups=[list(range(NCORES))],
            ins=[g_vin.ap().opt()], outs=[g_vout.ap().opt()]))
        gpsimd.wait_ge(sCC.h, 2)
        for r in range(NCORES):
            sGV[r].inc(gpsimd.dma_start(
                out=Vall[:, r * TPC:(r + 1) * TPC, :],
                in_=g_vout.ap()[r * ROWS:(r + 1) * ROWS, :].rearrange(
                    "(g p) e -> p g e", p=128)), 16)
        for bq in range(2, 4):
            sW1b[bq].inc(gpsimd.dma_start(
                out=W1T[:, 8 * bq:8 * (bq + 1), :, :, :].rearrange(
                    "p a b c d -> p (a b c d)"),
                in_=d_w1t[:, bq * 8192:(bq + 1) * 8192]), 16)

    @block.vector
    def _(vector):
        vector.wait_ge(sSm.h, 16 * 6)
        vector.tensor_copy(b1q[:, :], b1f8[:, :])
        vector.memset(ones8[0:1, :], 1.0)
        vector.memset(onesbf[0:1, 0:1], 1.0)
        sPrep.inc(vector.memset(Vloc[:, :, 64:65], 1.0))

        def front(t):
            B = 8 * (t % 4)
            vector.wait_ge(sXd[t % 6].h, 16 * (t // 6 + 1))
            if t >= 4:
                vector.wait_ge(sRst.h, t - 3)
            if t >= 4:
                vector.wait_ge(sTr.h, t - 3)
            sMn.inc(vector.tensor_scalar(xn_sc[t % 4][:, :], xbuf[t % 6][:, :],
                                         -1.0 / D, None, ALU.mult, ALU.add,
                                         accum_out=stat[:, B + 2:B + 3]))

        def mid(u):
            B = 8 * (u % 4)
            vector.wait_ge(sSq.h, u + 1)
            vector.tensor_scalar(stat[:, B + 5:B + 6], stat[:, B + 1:B + 2],
                                 1.0 / D, 1e-5, ALU.mult, ALU.add)
            vector.drain()
            sVr.inc(vector.reciprocal(stat[:, B + 6:B + 7], stat[:, B + 5:B + 6]))

        def xnst(u):
            B = 8 * (u % 4)
            vector.wait_ge(sRst.h, u + 1)
            sXn.inc(vector.tensor_scalar(xn_sc[u % 4][:, :], xbuf[u % 6][:, :],
                                         stat[:, B + 2:B + 3], stat[:, B + 7:B + 8],
                                         ALU.add, ALU.mult))

        def evac(u):
            own = u >= TPC
            tl = u % TPC
            vector.wait_ge(sEvac.h, u + 1)
            if own:
                TRB = PA[:, 0:1024] if u % 2 == 0 else PC[:, 1024:2048]
                wq_ = tl // 4
                tp = (tl % 4) * 128
                sEvV.inc(vector.tensor_copy(
                    xnT8[:, wq_, :, :, tp:tp + 128],
                    TRB[:, :].rearrange("p (j i c) -> p j i c", j=4, i=2)))
            if u == 7:
                vector.wait_ge(sProj.h, 8)
                sPev.inc(vector.tensor_copy(Vloc[:, 0:8, 0:64],
                                            PB[:, 0:512].rearrange("p (g e) -> p g e", g=8)))
            if u == 15:
                vector.wait_ge(sProj.h, 16)
                sPev.inc(vector.tensor_copy(Vloc[:, 8:16, 0:64],
                                            PB[:, 0:512].rearrange("p (g e) -> p g e", g=8)))
                vector.tensor_copy(KTloc[:, 0:1024], PC[0:HD, 0:1024])
                sPev.inc(vector.tensor_copy(KTloc[:, 1024:2048], PC[64:128, 0:1024]))

        for t in range(32):
            if t >= 2:
                xnst(t - 2)
            if t >= 3:
                evac(t - 3)
            front(t)
            if t >= 1:
                mid(t - 1)
        xnst(30)
        evac(29)
        mid(31)
        xnst(31)
        evac(30)
        evac(31)
        vector.wait_ge(sProj.h, 32)
        vector.tensor_copy(QT2[0:HD, 0:1024], PC[0:HD, 0:1024])
        vector.tensor_copy(QT2[0:HD, 1024:2048], PC[64:128, 0:1024])
        vector.tensor_copy(QT2[HD:128, 0:1024], PC[0:HD, 0:1024])
        sPev.inc(vector.tensor_copy(QT2[HD:128, 1024:2048], PC[64:128, 0:1024]))

    @block.scalar
    def _(scalar):
        def sq(t):
            B = 8 * (t % 4)
            scalar.wait_ge(sMn.h, t + 1)
            sSq.inc(scalar.activation(xn_sc[t % 4][:, :], xbuf[t % 6][:, :], AF.Square,
                                      bias=stat[:, B + 2:B + 3],
                                      accum_out=stat[:, B + 1:B + 2]))

        def rt(u):
            B = 8 * (u % 4)
            scalar.wait_ge(sVr.h, u + 1)
            sRst.inc(scalar.activation(stat[:, B + 7:B + 8], stat[:, B + 6:B + 7], AF.Sqrt))

        def evs(u):
            TRB = PA[:, 0:1024] if u % 2 == 0 else PC[:, 1024:2048]
            scalar.wait_ge(sTr.h, u + 1)
            if u >= 2:
                scalar.wait_ge(sProj.h, u - 1)
            scalar.activation(
                xnTt[u % 2][:, 0:4, :].rearrange("p a b -> p (a b)"),
                TRB[:, 0:512], AF.Identity)
            sEvac.inc(scalar.activation(
                xnTt[u % 2][:, 4:8, :].rearrange("p a b -> p (a b)"),
                TRB[:, 512:1024], AF.Identity))

        for t in range(32):
            sq(t)
            if t >= 1:
                rt(t - 1)
            if t >= 2:
                evs(t - 2)
        rt(31)
        evs(30)
        evs(31)

    @block.tensor
    def _(tensor):
        tensor.wait_ge(sPrep.h, 1)
        tensor.wait_ge(sWdma.h, 32)
        for t in range(33):
            if t < 32:
                TRB = PA[:, 0:1024] if t % 2 == 0 else PC[:, 1024:2048]
                tensor.wait_ge(sXn.h, t + 1)
                if t >= 2:
                    tensor.wait_ge(sEvac.h, t - 1)
                    if t - 2 >= TPC:
                        tensor.wait_ge(sEvV.h, t - 1 - TPC)
                for db in range(DB):
                    ins = tensor.transpose(
                        TRB[:, db * 128:(db + 1) * 128],
                        xn_sc[t % 4][:, db * 128:(db + 1) * 128], identf[:, :])
                    if db == DB - 1:
                        sTr.inc(ins)
            if t < 1:
                continue
            u = t - 1
            own = u >= TPC
            tl = u % TPC
            tensor.wait_ge(sEvac.h, u + 1)
            half = (tl % TPC) >= 8
            pdst = PC[64:128, (tl % 8) * 128:(tl % 8) * 128 + 128] if half \
                else PC[0:HD, tl * 128:(tl + 1) * 128]
            tpos = (0, 64) if half else None
            if not own:
                if u == 8:
                    tensor.wait_ge(sPev.h, 1)
                for db in range(DB):
                    tensor.matmul(pdst, wkb[:, db, :], xnTt[u % 2][:, db, :],
                                  start=(db == 0), stop=(db == DB - 1),
                                  tile_position=tpos)
                for db in range(DB):
                    ins = tensor.matmul(PB[:, (tl % 8) * 64:(tl % 8) * 64 + 64],
                                        xnTt[u % 2][:, db, :], wv1b[:, db, :],
                                        start=(db == 0), stop=(db == DB - 1))
                    if db == DB - 1:
                        sProj.inc(ins)
            else:
                if u == TPC:
                    tensor.wait_ge(sPev.h, 3)
                for db in range(DB):
                    ins = tensor.matmul(pdst, wqb[:, db, :], xnTt[u % 2][:, db, :],
                                        start=(db == 0), stop=(db == DB - 1),
                                        tile_position=tpos)
                    if db == DB - 1:
                        sProj.inc(ins)

    PEV_ALL = sPev.n
    WDMA_ALL = sWdma.n
    EV_ALL = sEvac.n

    ga1.close()

    gh = contextlib.ExitStack()

    def sbh(name, shape, dt):
        return gh.enter_context(nc.sbuf_tensor(name, shape, dt))

    W2T = sbh("W2T", [128, 16, 2, 2, 512], FP8)
    h8w = sbh("h8w", [128, 2, 16, 4, 2, 128], FP8)
    hf16 = sbh("hf16", [128, 2, 2, 512], FP16)
    NG2 = 64
    SLOT = [0, 1, 0, 1]

    @block.gpsimd
    def _(gpsimd):
        gpsimd.wait_ge(sSq.h, 32)
        gpsimd.wait_ge(sXn.h, 32)
        gpsimd.wait_ge(sTr.h, 32)
        gpsimd.wait_ge(sEvac.h, 32)
        gpsimd.wait_ge(sEvV.h, TPC)
        gpsimd.wait_ge(sProj.h, 32)
        gpsimd.wait_ge(sPev.h, 4)
        gpsimd.wait_ge(sGin.h, 32)
        for q in range(4):
            sW2.inc(gpsimd.dma_start(
                out=W2T[:, 4 * q:4 * (q + 1), :, :, :].rearrange(
                    "p a b c d -> p (a b c d)"),
                in_=d_w2t[:, q * 8192:(q + 1) * 8192]), 16)

    piters = []
    itcum = [0] * 66
    for q in range(64):
        lo0 = 128 * ((2 * q) >> 3)
        for (a, b) in _pieces(lo0, 2048):
            piters.append((q, a, b, a == lo0))
        itcum[q + 1] = len(piters)
    NIT = len(piters)
    OEV_WAIT = [itcum[16 * j + 15 + 1] for j in range(4)]

    @block.tensor
    def _(tensor):
        tensor.wait_ge(sPev.h, PEV_ALL)
        tensor.wait_ge(sEvac.h, EV_ALL)
        tensor.wait_ge(sEvV.h, TPC)
        B3 = [PA[:, 0:512], PA[:, 512:1024], PB[:, 0:512]]
        psH = PB[:, 512:1024]
        gq = 0
        seenK = set()
        seenV = set()

        def ffn1_p2(due):
            nonlocal gq
            while gq < due:
                w, hb = gq // HB, gq % HB
                tensor.wait_ge(sW1b[hb // 8].h, 16)
                if gq >= 1:
                    tensor.wait_ge(sHeV.h, gq)
                for j in range(4):
                    ins = tensor.matmul(psH, W1T[:, hb, j, :, :], xnT8[:, w, j, :, :],
                                        start=(j == 0), stop=(j == 3),
                                        perf_mode=mybir.MatmulPerfMode.DoubleRow)
                sH.inc(ins)
                gq += 1

        ffn1_p2(8)
        for it, (q, a, b, diag) in enumerate(piters):
            if q // 8 not in seenK:
                tensor.wait_ge(sGK[q // 8].h, 32)
                seenK.add(q // 8)
            for eo in range(2):
                s = 2 * it + eo
                if s >= 3:
                    tensor.wait_ge(sE.h, s - 2)
                sS.inc(tensor.matmul(
                    B3[s % 3][:, 0:b - a],
                    KTall2[eo * HD:eo * HD + HD, 128 * q:128 * (q + 1)],
                    QT2[eo * HD:eo * HD + HD, a:b], start=True, stop=True))
            for av_it in ([it - 1] if it >= 1 else []) + ([NIT - 1] if it == NIT - 1 else []):
                (q2, a2, b2, diag2) = piters[av_it]
                if q2 // 8 not in seenV:
                    tensor.wait_ge(sGV[q2 // 8].h, 16)
                    seenV.add(q2 // 8)
                tensor.wait_ge(sE.h, 2 * (av_it + 1))
                if diag2:
                    tensor.wait_ge(sMk.h, q2 + 1)
                for eo in range(2):
                    stop = (eo == 1 and q2 == 16 * (a2 // 512) + 15)
                    ins = tensor.matmul(
                        PC[0:65, a2:b2], Vall[:, 2 * q2 + eo, :],
                        Pt4[2 * (q2 % 2) + eo][:, a2:b2],
                        start=(q2 == 0 and eo == 0), stop=stop)
                sAV.inc(ins)
            ffn1_p2(min(NG2, ((it + 1) * NG2) // (NIT - 1)))

    @block.scalar
    def _(scalar):
        B3 = [PA[:, 0:512], PA[:, 512:1024], PB[:, 0:512]]
        for it, (q, a, b, diag) in enumerate(piters):
            if q >= 2:
                scalar.wait_ge(sAV.h, itcum[q - 1])
            for eo in range(2):
                s = 2 * it + eo
                scalar.wait_ge(sS.h, s + 1)
                sE.inc(scalar.activation(
                    Pt4[2 * (q % 2) + eo][:, a:b], B3[s % 3][:, 0:b - a],
                    AF.Exp, bias=0.0, scale=float(HD) ** -0.5))

    @block.vector
    def _(vector):
        vector.wait_ge(sWdma.h, WDMA_ALL)
        gv = 0

        def emit_tasks(due):
            nonlocal gv
            while gv < due:
                w, hb = gv // HB, gv % HB
                vector.wait_ge(sH.h, gv + 1)
                sHeV.inc(vector.tensor_scalar(
                    hf16[:, (hb // 2) % 2, hb % 2, :], PB[:, 512:1024],
                    b1q[:, hb:hb + 1], 0.0, ALU.add, ALU.max))
                if hb % 2 == 1:
                    j = hb // 2
                    vector.drain()
                    for i in range(2):
                        sH8V.inc(vector.tensor_copy(
                            h8w[:, SLOT[w], j, :, i, :],
                            hf16[:, j % 2, i, :].rearrange("p (tl m) -> p tl m", m=128)))
                gv += 1

        emit_tasks(8)
        for it, (q, a, b, diag) in enumerate(piters):
            if diag:
                m = (2 * q) >> 3
                vector.wait_ge(sE.h, 2 * (it + 1))
                vector.tensor_tensor(
                    Pt4[2 * (q % 2)][:, 128 * m:128 * m + 128],
                    Pt4[2 * (q % 2)][:, 128 * m:128 * m + 128],
                    maskc[:, (2 * q) & 7, :], ALU.mult)
                sMk.inc(vector.tensor_tensor(
                    Pt4[2 * (q % 2) + 1][:, 128 * m:128 * m + 128],
                    Pt4[2 * (q % 2) + 1][:, 128 * m:128 * m + 128],
                    maskc[:, (2 * q + 1) & 7, :], ALU.mult))
            emit_tasks(min(NG2, (it * NG2) // (NIT - 1)))
        emit_tasks(NG2)
        for j in range(4):
            vector.wait_ge(sAV.h, OEV_WAIT[j])
            vector.tensor_copy(lrow[0:1, 512 * j:512 * (j + 1)],
                               PC[64:65, 512 * j:512 * (j + 1)])
            sOev.inc(vector.tensor_copy(Obuf[:, 512 * j:512 * (j + 1)],
                                        PC[0:65, 512 * j:512 * (j + 1)]))

    gb = contextlib.ExitStack()

    def sbb(name, shape, dt):
        return gb.enter_context(nc.sbuf_tensor(name, shape, dt))

    yf16 = [sbb(f"yf16{i}", [128, 512], FP16) for i in range(2)]
    dmbuf = [sbb(f"dmbuf{i}", [128, D], BF16) for i in range(2)]
    cmb = [sbb(f"cmb{i}", [128, D], F32) for i in range(2)]
    tmp2 = [sbb(f"tmp2{i}", [128, 512], F32) for i in range(2)]

    @block.tensor
    def _(tensor):
        tensor.wait_ge(sE.h, 2 * NIT)
        tensor.wait_ge(sHeV.h, NG2)
        tensor.wait_ge(sOev.h, 4)
        for t in range(TPC):
            ins = tensor.matmul(PA[:, t:t + 1], lrow[0:1, t * 128:(t + 1) * 128],
                                onesbf[0:1, 0:1], start=True, stop=True)
            if t == TPC - 1:
                sL.inc(ins)

        psH3 = [PA[:, 0:512], PA[:, 512:1024], PC[:, 1536:2048]]
        psY3 = [PB[:, 0:512], PB[:, 512:1024], PC[:, 1024:1536]]
        g3 = 0

        def ffn1_groups(tensor, k):
            nonlocal g3
            for _ in range(k):
                if g3 >= 2 * HB:
                    return
                wv, hb = 2 + g3 // HB, g3 % HB
                if g3 < 2:
                    tensor.wait_ge(sRc.h, 1)
                elif g3 >= 3:
                    tensor.wait_ge(sHe.h, g3 - 2)
                for j in range(4):
                    ins = tensor.matmul(
                        psH3[g3 % 3], W1T[:, hb, j, :, :], xnT8[:, wv, j, :, :],
                        start=(j == 0), stop=(j == 3),
                        perf_mode=mybir.MatmulPerfMode.DoubleRow)
                sH.inc(ins)
                g3 += 1

        for w in range(4):
            for tl in range(4):
                tg = 4 * w + tl
                for nb in range(2):
                    n = 2 * tg + nb
                    if n >= 8:
                        ffn1_groups(tensor, 4)
                    if w < 2:
                        tensor.wait_ge(sH8V.h, 32 * (w + 1))
                    else:
                        tensor.wait_ge(sH8.h, 32 * (w - 1))
                    if n == 0:
                        tensor.wait_ge(sW2.h, 64)
                    if n >= 3:
                        tensor.wait_ge(sYe.h, n - 2)
                    for j in range(16):
                        tensor.matmul(psY3[n % 3],
                                      h8w[:, SLOT[w], j, tl, :, :],
                                      W2T[:, j, nb, :, :],
                                      start=(j == 0), stop=False,
                                      perf_mode=mybir.MatmulPerfMode.DoubleRow)
                    sF2.inc(tensor.matmul(psY3[n % 3],
                                          ones8[0:1, 0:128],
                                          b2q8[0:1, nb * 512:(nb + 1) * 512],
                                          start=False, stop=True))
                    if n >= 2:
                        tensor.wait_ge(sCb.h, n - 1)
                    sA3.inc(tensor.matmul(PC[:, (n % 2) * 512:(n % 2) * 512 + 512],
                                          Obuf[0:HD, tg * 128:(tg + 1) * 128],
                                          wv2b[0:HD, nb * 512:(nb + 1) * 512],
                                          start=True, stop=True))

    @block.vector
    def _(vector):
        vector.wait_ge(sL.h, 1)
        sRc.inc(vector.reciprocal(rcols[:, :], PA[:, 0:TPC]))
        vector.drain()
        j3 = 0
        for w in range(4):
            for tl in range(4):
                tg = 4 * w + tl
                for nb in range(2):
                    n = 2 * tg + nb
                    for _ in range(2 if n >= 9 else 0):
                        if j3 < 32:
                            vector.wait_ge(sHe.h, 2 * j3 + 2)
                            if j3 % 16 == 0:
                                vector.wait_ge(sF2.h, 8 * (j3 // 16 + 1))
                            for i in range(2):
                                sH8.inc(vector.tensor_copy(
                                    h8w[:, j3 // 16, j3 % 16, :, i, :],
                                    hf16[:, j3 % 2, i, :].rearrange(
                                        "p (tl m) -> p tl m", m=128)))
                            j3 += 1
                    vector.wait_ge(sA3.h, n + 1)
                    vector.wait_ge(sYe.h, n + 1)
                    vector.wait_ge(sDm2[tg % 2].h, 32 * (tg // 2 + 1))
                    if tg >= 2:
                        vector.wait_ge(sOut2[tg % 2].h, 16 * (tg // 2))
                    vector.scalar_tensor_tensor(
                        cmb[tg % 2][:, nb * 512:(nb + 1) * 512],
                        PC[:, (n % 2) * 512:(n % 2) * 512 + 512],
                        rcols[:, tg:tg + 1],
                        cmb[tg % 2][:, nb * 512:(nb + 1) * 512],
                        ALU.mult, ALU.add)
                    vector.tensor_tensor(tmp2[n % 2][:, :], yf16[n % 2][:, :],
                                         dmbuf[tg % 2][:, nb * 512:(nb + 1) * 512],
                                         ALU.mult)
                    vector.drain()
                    sCb.inc(vector.tensor_tensor(
                        cmb[tg % 2][:, nb * 512:(nb + 1) * 512],
                        cmb[tg % 2][:, nb * 512:(nb + 1) * 512],
                        tmp2[n % 2][:, :], ALU.add))

    @block.scalar
    def _(scalar):
        scalar.wait_ge(sH8V.h, NG2)
        psH3 = [PA[:, 0:512], PA[:, 512:1024], PC[:, 1536:2048]]
        psY3 = [PB[:, 0:512], PB[:, 512:1024], PC[:, 1024:1536]]
        g3 = 0
        for w in range(4):
            for tl in range(4):
                for nb in range(2):
                    n = 2 * (4 * w + tl) + nb
                    for _ in range(4 if n >= 8 else 0):
                        if g3 < 2 * HB:
                            scalar.wait_ge(sH.h, NG2 + g3 + 1)
                            if g3 >= 4:
                                scalar.wait_ge(sH8.h, 2 * (g3 // 2) - 2)
                            sHe.inc(scalar.activation(
                                hf16[:, (g3 // 2) % 2, g3 % 2, :],
                                psH3[g3 % 3],
                                AF.Relu, bias=b1q[:, (g3 % HB):(g3 % HB) + 1]))
                            g3 += 1
                    scalar.wait_ge(sF2.h, n + 1)
                    if n >= 2:
                        scalar.wait_ge(sCb.h, n - 1)
                    sYe.inc(scalar.activation(yf16[n % 2][:, :],
                                              psY3[n % 3],
                                              AF.Identity))

    @block.gpsimd
    def _(gpsimd):
        for tg in range(TPC):
            if tg >= 2:
                gpsimd.wait_ge(sCb.h, 2 * (tg - 1))
                gpsimd.wait_ge(sOut2[tg % 2].h, 16 * (tg // 2))
            sDm2[tg % 2].inc(gpsimd.dma_start(out=dmbuf[tg % 2][:, :],
                                     in_=d_dm[tg * 128:(tg + 1) * 128, :]), 16)
            sDm2[tg % 2].inc(gpsimd.dma_start(out=cmb[tg % 2][:, :],
                                     in_=d_xown[tg * 128:(tg + 1) * 128, :]), 16)

    @block.sync
    def _(sync):
        for tg in range(TPC):
            sync.wait_ge(sCb.h, 2 * (tg + 1))
            sOut2[tg % 2].inc(sync.dma_start(out=d_out[tg * 128:(tg + 1) * 128, :],
                                             in_=cmb[tg % 2][:, :]), 16)
        sync.wait_ge(sOut2[0].h, 16 * (TPC // 2))
        sync.wait_ge(sOut2[1].h, 16 * (TPC // 2))

    gb.close()
    gh.close()
    ga2.close()
    return nc


_NC_CACHE = None


def _get_nc():
    global _NC_CACHE
    if _NC_CACHE is None:
        _NC_CACHE = build_nc()
    return _NC_CACHE


def kernel(L0_attn_x, L0_attn_w_q, L0_attn_w_k, L0_attn_w_v1, L0_attn_w_v2,
           L0_ffn_w1, L0_ffn_b1, L0_ffn_w2, L0_ffn_b2):
    x = np.ascontiguousarray(np.asarray(L0_attn_x, dtype=np.float32).reshape(S, D))
    wq = np.ascontiguousarray(np.asarray(L0_attn_w_q, dtype=np.float32))
    wk = np.ascontiguousarray(np.asarray(L0_attn_w_k, dtype=np.float32))
    wv1 = np.ascontiguousarray(np.asarray(L0_attn_w_v1, dtype=np.float32))
    wv2 = np.ascontiguousarray(np.asarray(L0_attn_w_v2, dtype=np.float32))
    w1 = np.ascontiguousarray(np.asarray(L0_ffn_w1, dtype=np.float32))
    b1 = np.asarray(L0_ffn_b1, dtype=np.float32)
    w2 = np.ascontiguousarray(np.asarray(L0_ffn_w2, dtype=np.float32))
    b2 = np.asarray(L0_ffn_b2, dtype=np.float32)

    w1dr = np.ascontiguousarray(
        w1.reshape(HB, 128, 4, 2, 128).transpose(4, 0, 2, 3, 1).reshape(128, -1))
    w2dr = np.ascontiguousarray(
        w2.reshape(2, 512, 16, 2, 128).transpose(4, 2, 0, 3, 1).reshape(128, -1))
    b1r = np.ascontiguousarray(b1.reshape(HB, 128).T)
    b2r = np.ascontiguousarray(b2.reshape(1, D))
    ident = np.eye(128, dtype=np.float32)

    import jax
    with jax.default_device(jax.devices("cpu")[0]):
        keep = np.asarray(jax.random.bernoulli(
            jax.random.key(42, impl="threefry2x32"), 0.8, (1, S, D)))
    dmf = np.where(keep.reshape(S, D), np.float32(1.25), np.float32(0.0))
    dm_all = dmf.astype(ml_dtypes.bfloat16)

    x_tiles = x.reshape(S // 128, 128, D)
    dm_tiles = dm_all.reshape(S // 128, 128, D)

    in_maps = []
    for c in range(NCORES):
        own = [8 * t + c for t in range(TPC)]
        p_idx = np.arange(128)[:, None, None]
        r_idx = np.arange(8)[None, :, None]
        j_idx = np.arange(128)[None, None, :]
        mc = ((128 * r_idx + p_idx - 128 * c) <= j_idx).astype(ml_dtypes.bfloat16)
        in_maps.append({
            "x_prod": np.ascontiguousarray(x[ROWS * c:ROWS * (c + 1)]),
            "x_own": np.ascontiguousarray(x_tiles[own].reshape(ROWS, D)),
            "wq": wq, "wk": wk, "wv1": wv1, "wv2": wv2,
            "w1t": w1dr, "b1r": b1r,
            "w2t": w2dr, "b2r": b2r,
            "maskc": np.ascontiguousarray(mc),
            "dm": np.ascontiguousarray(dm_tiles[own].reshape(ROWS, D)),
            "ident": ident,
        })

    nc = _get_nc()
    res = run_bass_kernel_spmd(nc, in_maps, core_ids=list(range(NCORES)))

    out = np.empty((S // 128, 128, D), dtype=np.float32)
    for c in range(NCORES):
        own = [8 * t + c for t in range(TPC)]
        out[own] = np.asarray(res.results[c]["out"], dtype=np.float32).reshape(TPC, 128, D)
    return out.reshape(1, S, D)
